# version 06
# baseline (speedup 1.0000x reference)
import dataclasses
import numpy as np
import concourse.bass as bass
import concourse.mybir as mybir
from contextlib import ExitStack

F32 = mybir.dt.float32
F16 = mybir.dt.float16
I32 = mybir.dt.int32
AF = mybir.ActivationFunctionType
ALU = mybir.AluOpType

T = 4096
D = 2048
DI = 4096
H = 32
HP = 128
NS = 128
Q = 128
NCH = T // Q
KC = D // 128
TT = 512
NTT = T // TT
FN = 8192
RMS_EPS = 1e-6
LN_EPS = 1e-5

IN_BLOCKS = ([("dt", 2 * DI, 32), ("B", 2 * DI + 32, 128), ("C", 2 * DI + 160, 128)]
             + [(f"ssm{f}", DI + f * 128, 128) for f in range(32)]
             + [(f"gate{f}", f * 128, 128) for f in range(32)])


def block_col_offsets():
    offs = {}
    o = 0
    for (name, n0, ncols) in IN_BLOCKS:
        offs[name] = (o, ncols)
        o += KC * ncols
    return offs, o


class SemCtr:
    def __init__(self, nc, stk, name):
        self.h = stk.enter_context(nc.semaphore(name))
        self.n = 0

    def inc(self, instr, k=1):
        instr.then_inc(self.h, k)
        self.n += k
        return instr

    def wait(self, eng, val=None):
        eng.wait_ge(self.h, self.n if val is None else val)


class DBuf:

    def __init__(self, nc, stk, name, shape, dtype, n=2, fill_inc=16):
        self.bufs = [stk.enter_context(nc.sbuf_tensor(f"{name}{i}", shape, dtype)) for i in range(n)]
        self.fill = [stk.enter_context(nc.semaphore(f"{name}_f{i}")) for i in range(n)]
        self.free = stk.enter_context(nc.semaphore(f"{name}_e"))
        self.nb = n
        self.fill_inc = fill_inc
        self.loads = 0
        self.uses = 0

    def prod_buf(self):
        return self.bufs[self.loads % self.nb]

    def prod_wait(self, eng):
        if self.loads >= self.nb:
            eng.wait_ge(self.free, self.loads - self.nb + 1)

    def prod_done(self, instr):
        instr.then_inc(self.fill[self.loads % self.nb], self.fill_inc)
        self.loads += 1

    def cons_wait(self, eng):
        slot = self.uses % self.nb
        eng.wait_ge(self.fill[slot], self.fill_inc * (self.uses // self.nb + 1))

    def cons_buf(self):
        return self.bufs[self.uses % self.nb]

    def cons_done(self, instr):
        instr.then_inc(self.free, 1)
        self.uses += 1


dbg_hold = []


def build_program(debug=(), phases=("p0", "p1a", "prep", "p1b", "hand", "p2a", "p2b")):
    nc = bass.Bass("TRN2", target_bir_lowering=False, debug=False,
                   detect_race_conditions=False)
    BLK_OFF, WI_COLS = block_col_offsets()

    def dram(name, shape, dtype, kind="Internal", **kw):
        if kind == "Internal" and name in debug:
            kind = "ExternalOutput"
        return nc.dram_tensor(name, shape, dtype, kind=kind, **kw).ap()

    x_in = dram("x", [D, T], F32, kind="ExternalInput")
    x3_in = dram("x3", [D, 3], F32, kind="ExternalInput")
    wi_in = dram("wi", [128, WI_COLS], F32, kind="ExternalInput")
    w0_in = dram("w0", [128, 64 * KC * 128], F32, kind="ExternalInput")
    w1_in = dram("w1", [128, 16 * 64 * 128], F32, kind="ExternalInput")
    wo_in = dram("wo", [128, 16 * 32 * 128], F32, kind="ExternalInput")
    convw_in = dram("convw", [128, 32 * 4], F32, kind="ExternalInput")
    convb_in = dram("convb", [128, 32], F32, kind="ExternalInput")
    dvec_in = dram("dvec", [128, 32], F32, kind="ExternalInput")
    scale_in = dram("scale", [128, KC], F32, kind="ExternalInput")
    b0_in = dram("b0", [128, 64], F32, kind="ExternalInput")
    b1_in = dram("b1", [128, 16], F32, kind="ExternalInput")
    alog_in = dram("alog", [32, 1], F32, kind="ExternalInput")
    dtb_in = dram("dtb", [32, 1], F32, kind="ExternalInput")
    ctl_in = dram("ctl", [1, 8], I32, kind="ExternalInput")
    noncef_in = dram("noncef", [1, 64], F32, kind="ExternalInput")
    msel_in = dram("msel", [1, DI], F32, kind="ExternalInput")
    out_o = dram("out", [D, T], F32, kind="ExternalOutput")

    xn_sp = dram("xn_sp", [D, T], F16)
    xc_sp = dram("xc_sp", [DI, T], F16)
    sg_sp = dram("sg_sp", [DI, T], F16)
    yl_sp = dram("yl_sp", [DI, T], F16)
    d0_sp = dram("d0_sp", [FN, T], F16)
    g_sp = dram("g_sp", [D, T], F16)
    w116_sp = dram("w116_sp", [128, 16 * 64 * 128], F16)
    wo16_sp = dram("wo16_sp", [128, 16 * 32 * 128], F16)
    cum_cm = dram("cum_cm", [NCH, H * Q], F32)
    e_cm = dram("e_cm", [NCH, H * Q], F16)
    egl_tm = dram("egl_tm", [NTT, H * TT], F16)
    drep_sp = dram("drep_sp", [1, NCH * H], F32)
    dt_sp = dram("dt_sp", [32, T], F32)
    dbg_sp = dram("dbg_sp", [128, T], F32)
    shs = dram("shs", [256, DI], F32, addr_space="Shared")
    shf = dram("shf", [2, 64], F32, addr_space="Shared")

    g = nc.gpsimd
    sy = nc.sync
    t = nc.tensor
    v = nc.vector
    s = nc.scalar
    ALL_ENG = [t, v, s, g, sy]

    with ExitStack() as stk:
        def sb(name, shape, dtype):
            return stk.enter_context(nc.sbuf_tensor(name, shape, dtype))

        def sem(name):
            return SemCtr(nc, stk, name)

        ones16 = sb("ones16", [128, 128], F16)
        ident16 = sb("ident16", [128, 128], F16)
        ident32 = sb("ident32", [128, 128], F32)
        trimask = sb("trimask", [128, 128], F16)
        scale_sb = sb("scale_sb", [128, KC], F32)
        convw_sb = sb("convw_sb", [128, 32 * 4], F32)
        convb_sb = sb("convb_sb", [128, 32], F32)
        dvec_sb = sb("dvec_sb", [128, 32], F32)
        b0_sb = sb("b0_sb", [128, 64], F32)
        b1_sb = sb("b1_sb", [128, 16], F32)
        alog_sb = sb("alog_sb", [32, 1], F32)
        dtb_sb = sb("dtb_sb", [32, 1], F32)
        ctl_sb = sb("ctl_sb", [1, 8], I32)
        nonce_sb = sb("nonce_sb", [1, 64], F32)
        flag_sb = sb("flag_sb", [1, 64], F32)
        C16 = sb("C16", [128, T], F16)
        cst = sb("cst", [128, 2], F32)
        acol = sb("acol", [32, 1], F32)
        S16 = sb("S16", [128, DI], F16)
        S_holder = {}
        p1stk = stk.enter_context(ExitStack())
        B16 = p1stk.enter_context(nc.sbuf_tensor("B16", [128, T], F16))
        B16_holder = {"B16": B16, "stk": p1stk}
        hstk = stk.enter_context(ExitStack())
        h_sb = hstk.enter_context(nc.sbuf_tensor("h_sb", [128, KC * T], F16))

        bar = sem("bar")

        eng_ctr = {}

        def barrier():
            for e in ALL_ENG:
                c = eng_ctr.get(id(e))
                if c is not None:
                    e.wait_ge(c.h, c.n)
            for e in ALL_ENG:
                bar.inc(e.nop(), 1)
            for e in ALL_ENG:
                e.wait_ge(bar.h, bar.n)

        ld = sem("ld")
        gld = sem("gld")
        pe = sem("pe")
        dv = sem("dv")
        ac = sem("ac")
        gp = sem("gp")
        eng_ctr[id(t)] = pe
        eng_ctr[id(v)] = dv
        eng_ctr[id(s)] = ac
        eng_ctr[id(g)] = gp

        for (ap_in, tile) in [(scale_in, scale_sb), (convw_in, convw_sb), (convb_in, convb_sb),
                              (dvec_in, dvec_sb), (b0_in, b0_sb), (b1_in, b1_sb),
                              (alog_in, alog_sb), (dtb_in, dtb_sb), (ctl_in, ctl_sb),
                              (noncef_in, nonce_sb)]:
            ld.inc(sy.dma_start(out=tile[:, :], in_=ap_in), 16)
        dv.inc(v.memset(ones16[:, :], 1.0), 1)
        v.memset(cst[:, 0:1], LN_EPS)
        dv.inc(v.memset(cst[:, 1:2], RMS_EPS), 1)
        gp.inc(g.memset(ident16[:, :], 0.0), 1)
        g.wait_ge(gp.h, gp.n)
        gp.inc(g.affine_select(out=ident16[:, :], in_=ident16[:, :], compare_op=ALU.not_equal,
                               fill=1.0, base=0, pattern=[[-1, 128]], channel_multiplier=1), 1)
        g.wait_ge(gp.h, gp.n)
        gp.inc(g.memset(ident32[:, :], 0.0), 1)
        g.wait_ge(gp.h, gp.n)
        gp.inc(g.affine_select(out=ident32[:, :], in_=ident32[:, :], compare_op=ALU.not_equal,
                               fill=1.0, base=0, pattern=[[-1, 128]], channel_multiplier=1), 1)
        g.wait_ge(gp.h, gp.n)
        gp.inc(g.memset(trimask[:, :], 1.0), 1)
        g.wait_ge(gp.h, gp.n)
        gp.inc(g.affine_select(out=trimask[:, :], in_=trimask[:, :], compare_op=ALU.is_ge,
                               fill=0.0, base=0, pattern=[[1, 128]], channel_multiplier=-1), 1)
        g.wait_ge(gp.h, gp.n)
        gp.inc(g.engine_nop(), 1)
        s.wait_ge(ld.h, ld.n)
        ac.inc(s.activation(acol[:, :], alog_sb[:, :], AF.Exp), 1)
        v.wait_ge(ac.h, ac.n)
        dv.inc(v.tensor_scalar_mul(acol[:, :], acol[:, :], -1.0), 1)
        for e in ALL_ENG:
            e.wait_ge(ld.h, ld.n)
            e.wait_ge(gp.h, gp.n)
            e.wait_ge(dv.h, dv.n)

        if "p0" in phases:
            with ExitStack() as pk:
                def sbp(name, shape, dtype):
                    return pk.enter_context(nc.sbuf_tensor(name, shape, dtype))

                def psp(name, shape, dtype=F32):
                    return pk.enter_context(nc.psum_tensor(name, shape, dtype))
                x16 = sbp("x16", [128, KC * TT], F16)
                xsq = [sbp(f"xsq{i}", [128, TT], F16) for i in range(2)]
                mbc = sbp("mbc", [128, TT], F32)
                var = sbp("var", [128, TT], F32)
                tp1 = sbp("tp1", [128, TT], F32)
                tp2 = sbp("tp2", [128, TT], F32)
                s2bc = sbp("s2bc", [128, TT], F32)
                pss = psp("pss", [128, TT])
                psq = psp("psq", [128, TT])
                xl_f = sem("xl_f")
                sq_f = sem("sq_f")
                sqr_f = sem("sqr_f")
                st_f = sem("st_f")
                su_f = sem("su_f")
                xb_f = sem("xb_f")
                xw_f = sem("xw_f")

                for tt in range(NTT):
                    if tt >= 1:
                        g.wait_ge(xw_f.h, 16 * tt)
                    for kc in range(KC):
                        src = x_in[kc * 128:(kc + 1) * 128, tt * TT:(tt + 1) * TT]
                        xl_f.inc(g.dma_start(out=x16[:, kc * TT:(kc + 1) * TT], in_=src), 16)
                    if tt >= 1:
                        t.wait_ge(su_f.h, tt)
                    s.wait_ge(xl_f.h, 16 * KC * (tt + 1))
                    t.wait_ge(xl_f.h, 16 * KC * (tt + 1))
                    for kc in range(KC):
                        if tt * KC + kc >= 2:
                            s.wait_ge(sqr_f.h, tt * KC + kc - 1)
                        sq_f.inc(s.activation(xsq[kc % 2][:, :], x16[:, kc * TT:(kc + 1) * TT], AF.Square), 1)
                        t.wait_ge(sq_f.h, sq_f.n)
                        t.matmul(pss[:, :], ones16[:, :], x16[:, kc * TT:(kc + 1) * TT],
                                 start=(kc == 0), stop=(kc == KC - 1), skip_group_check=True)
                        sqr_f.inc(t.matmul(psq[:, :], ones16[:, :], xsq[kc % 2][:, :],
                                           start=(kc == 0), stop=(kc == KC - 1), skip_group_check=True), 1)
                    v.wait_ge(sqr_f.h, sqr_f.n)
                    dv.inc(v.tensor_scalar_mul(mbc[:, :], pss[:, :], 1.0 / D), 1)
                    s.wait_ge(dv.h, dv.n)
                    ac.inc(s.activation(tp1[:, :], mbc[:, :], AF.Square), 1)
                    v.wait_ge(ac.h, ac.n)
                    su_f.inc(v.scalar_tensor_tensor(var[:, :], psq[:, :], 1.0 / D, tp1[:, :],
                                                    ALU.mult, ALU.subtract), 1)
                    s.wait_ge(su_f.h, su_f.n)
                    s.activation(tp1[:, :], var[:, :], AF.Ln, bias=cst[:, 0:1])
                    s.activation(s2bc[:, :], tp1[:, :], AF.Exp, scale=-0.5)
                    ac.inc(s.activation(tp2[:, :], tp1[:, :], AF.Exp, scale=-1.0), 1)
                    v.wait_ge(ac.h, ac.n)
                    dv.inc(v.tensor_mul(tp1[:, :], var[:, :], tp2[:, :]), 1)
                    s.wait_ge(dv.h, dv.n)
                    s.activation(tp1[:, :], tp1[:, :], AF.Ln, bias=cst[:, 1:2])
                    ac.inc(s.activation(tp1[:, :], tp1[:, :], AF.Exp, scale=-0.5), 1)
                    v.wait_ge(ac.h, ac.n)
                    v.tensor_mul(s2bc[:, :], s2bc[:, :], tp1[:, :])
                    if tt >= 1:
                        v.wait_ge(xw_f.h, 16 * tt)
                    lastv = None
                    for kc in range(KC):
                        xsl = x16[:, kc * TT:(kc + 1) * TT]
                        v.tensor_sub(xsl, xsl, mbc[:, :])
                        lastv = v.tensor_mul(xsl, xsl, s2bc[:, :])
                        lasth = v.tensor_scalar_mul(h_sb[:, kc * T + tt * TT: kc * T + (tt + 1) * TT],
                                                    xsl, scale_sb[:, kc:kc + 1])
                    xb_f.inc(lastv, 1)
                    dv.inc(lasth, 1)
                    sy.wait_ge(xb_f.h, xb_f.n)
                    dst = xn_sp.rearrange("(kc p) t -> p kc t", p=128)[:, :, tt * TT:(tt + 1) * TT]
                    src2 = x16.ap().rearrange("p (kc i) -> p kc i", kc=KC)
                    xw_f.inc(sy.dma_start(out=dst, in_=src2), 16)

                sy.wait_ge(xw_f.h, 16 * NTT)
                barrier()

                x316 = sbp("x316", [128, KC * 3], F16)
                h3 = sbp("h3", [128, KC * 3], F16)
                xsq3 = sbp("xsq3", [128, 3], F16)
                st3 = sbp("st3", [128, 12], F32)
                for kc in range(KC):
                    gld.inc(g.dma_start(out=x316[:, kc * 3:(kc + 1) * 3],
                                        in_=x3_in[kc * 128:(kc + 1) * 128, :]), 16)
                for kc in range(KC):
                    s.wait_ge(gld.h, gld.n)
                    if kc >= 1:
                        s.wait_ge(pe.h, pe.n)
                    ac.inc(s.activation(xsq3[:, :], x316[:, kc * 3:(kc + 1) * 3], AF.Square), 1)
                    t.wait_ge(ac.h, ac.n)
                    t.wait_ge(gld.h, gld.n)
                    t.matmul(pss[:, 0:3], ones16[:, :], x316[:, kc * 3:(kc + 1) * 3],
                             start=(kc == 0), stop=(kc == KC - 1), skip_group_check=True)
                    pe.inc(t.matmul(psq[:, 0:3], ones16[:, :], xsq3[:, :],
                                    start=(kc == 0), stop=(kc == KC - 1), skip_group_check=True), 1)
                pe.inc(t.nop(), 1)
                m3, s23, t13, t23 = st3[:, 0:3], st3[:, 3:6], st3[:, 6:9], st3[:, 9:12]
                v.wait_ge(pe.h, pe.n)
                dv.inc(v.tensor_scalar_mul(m3, pss[:, 0:3], 1.0 / D), 1)
                s.wait_ge(dv.h, dv.n)
                ac.inc(s.activation(t13, m3, AF.Square), 1)
                v.wait_ge(ac.h, ac.n)
                dv.inc(v.scalar_tensor_tensor(t23, psq[:, 0:3], 1.0 / D, t13, ALU.mult, ALU.subtract), 1)
                s.wait_ge(dv.h, dv.n)
                s.activation(t13, t23, AF.Ln, bias=cst[:, 0:1])
                s.activation(s23, t13, AF.Exp, scale=-0.5)
                ac.inc(s.activation(t13, t13, AF.Exp, scale=-1.0), 1)
                v.wait_ge(ac.h, ac.n)
                dv.inc(v.tensor_mul(t13, t23, t13), 1)
                s.wait_ge(dv.h, dv.n)
                s.activation(t13, t13, AF.Ln, bias=cst[:, 1:2])
                ac.inc(s.activation(t13, t13, AF.Exp, scale=-0.5), 1)
                v.wait_ge(ac.h, ac.n)
                v.tensor_mul(s23, s23, t13)
                for kc in range(KC):
                    v.tensor_sub(x316[:, kc * 3:(kc + 1) * 3], x316[:, kc * 3:(kc + 1) * 3], m3)
                    v.tensor_mul(x316[:, kc * 3:(kc + 1) * 3], x316[:, kc * 3:(kc + 1) * 3], s23)
                    v.tensor_scalar_mul(h3[:, kc * 3:(kc + 1) * 3],
                                        x316[:, kc * 3:(kc + 1) * 3], scale_sb[:, kc:kc + 1])
                dv.inc(v.tensor_copy(S16[:, 0:KC * 3], h3[:, :]), 1)
            barrier()
        h3_view = S16[:, 0:KC * 3]

        if "p1a" in phases:
            with ExitStack() as pk:
                def sbp(name, shape, dtype):
                    return pk.enter_context(nc.sbuf_tensor(name, shape, dtype))

                def psp(name, shape, dtype=F32):
                    return pk.enter_context(nc.psum_tensor(name, shape, dtype))
                psA = [psp(f"psA{i}", [128, TT]) for i in range(8)]
                wbuf = DBuf(nc, pk, "wbuf", [128, KC * 128], F16, n=2)
                ssm_ext = sbp("ssm_ext", [128, 3 + T], F16)
                cacc = sbp("cacc", [128, T], F32)
                cstage = [sbp("cst0", [128, T], F16)]
                gstage = cstage
                tile_f = sem("tile_f")
                ev_f = sem("ev_f")
                cv_f = sem("cv_f")
                sw_f = sem("sw_f")
                gw_f = sem("gw_f")

                def load_block(name):
                    (o, ncols) = BLK_OFF[name]
                    wbuf.prod_wait(g)
                    wb = wbuf.prod_buf()
                    wbuf.prod_done(g.dma_start(out=wb[:, 0:KC * ncols],
                                               in_=wi_in[:, o:o + KC * ncols]))
                    return wb, ncols

                bank_free_at = [0] * 8

                def proj_block(wb, ncols, evac_fn, with_h3=False):
                    wbuf.cons_wait(t)
                    for tl in range(NTT):
                        t.wait_ge(ev_f.h, bank_free_at[tl])
                        for kc in range(KC):
                            tile_done = t.matmul(psA[tl][0:ncols, :], wb[:, kc * ncols:(kc + 1) * ncols],
                                                 h_sb[:, kc * T + tl * TT: kc * T + (tl + 1) * TT],
                                                 start=(kc == 0), stop=(kc == KC - 1),
                                                 skip_group_check=True)
                        tile_f.inc(tile_done, 1)
                    if with_h3:
                        t.wait_ge(ev_f.h, ev_f.n + 1)
                        for kc in range(KC):
                            mm3 = t.matmul(psA[0][0:ncols, 0:3], wb[:, kc * ncols:(kc + 1) * ncols],
                                           h3_view[:, kc * 3:(kc + 1) * 3],
                                           start=(kc == 0), stop=(kc == KC - 1),
                                           skip_group_check=True)
                        tile_f.inc(mm3, 1)
                    t.wait_ge(tile_f.h, tile_f.n)
                    wbuf.cons_done(t.nop())
                    for tl in range(NTT):
                        evac_fn(tl, psA[tl])
                        bank_free_at[tl] = ev_f.n
                    if with_h3:
                        bank_free_at[0] = ev_f.n + 1

                dtw_f = sem("dtw_f")
                for name, nc_ in [("dt", 32), ("B", 128), ("C", 128)]:
                    wb, ncols = load_block(name)
                    base_t = tile_f.n
                    dstt = {"dt": cacc, "B": B16, "C": C16}[name]

                    def ev(tl, ps, dstt=dstt, base_t=base_t, ncols=ncols):
                        v.wait_ge(tile_f.h, base_t + tl + 1)
                        ev_f.inc(v.tensor_copy(dstt[0:ncols, tl * TT:(tl + 1) * TT], ps[0:ncols, :]), 1)
                    proj_block(wb, ncols, ev)
                    if name == "dt":
                        sy.wait_ge(ev_f.h, ev_f.n)
                        dtw_f.inc(sy.dma_start(out=dt_sp, in_=cacc[0:32, :]), 16)

                for f in range(32):
                    wb, ncols = load_block(f"ssm{f}")
                    base_t = tile_f.n

                    def ev(tl, ps, base_t=base_t, f=f):
                        v.wait_ge(tile_f.h, base_t + tl + 1)
                        ev_f.inc(v.tensor_copy(ssm_ext[:, 3 + tl * TT:3 + (tl + 1) * TT], ps[:, :]), 1)
                    proj_block(wb, 128, ev, with_h3=True)
                    v.wait_ge(tile_f.h, tile_f.n)
                    ev_f.inc(v.tensor_copy(ssm_ext[:, 0:3], psA[0][:, 0:3]), 1)
                    if f == 0:
                        v.wait_ge(dtw_f.h, 16)
                    if f >= 1:
                        v.wait_ge(cv_f.h, f)
                    v.tensor_scalar(cacc[:, :], ssm_ext[:, 0:T], convw_sb[:, 4 * f:4 * f + 1], None, ALU.mult)
                    for k in range(1, 4):
                        ci = v.scalar_tensor_tensor(cacc[:, :], ssm_ext[:, k:k + T],
                                                    convw_sb[:, 4 * f + k:4 * f + k + 1], cacc[:, :],
                                                    ALU.mult, ALU.add)
                    dv.inc(ci, 1)
                    s.wait_ge(dv.h, dv.n)
                    if f >= 1:
                        s.wait_ge(sw_f.h, 16 * f)
                    stg = cstage[0]
                    cv_f.inc(s.activation(stg[:, :], cacc[:, :], AF.Silu, bias=convb_sb[:, f:f + 1]), 1)
                    sy.wait_ge(cv_f.h, cv_f.n)
                    sw_f.inc(sy.dma_start(out=xc_sp[f * 128:(f + 1) * 128, :], in_=stg[:, :]), 16)

                for f in range(32):
                    wb, ncols = load_block(f"gate{f}")
                    base_t = tile_f.n
                    stg = gstage[0]

                    def ev(tl, ps, base_t=base_t, f=f, stg=stg):
                        s.wait_ge(tile_f.h, base_t + tl + 1)
                        if tl == 0 and f == 0:
                            s.wait_ge(sw_f.h, 16 * 32)
                        if tl == 0 and f >= 1:
                            s.wait_ge(gw_f.h, 16 * f)
                        ins = s.activation(stg[:, tl * TT:(tl + 1) * TT], ps[:, :], AF.Silu)
                        ev_f.inc(ins, 1)
                        if tl == NTT - 1:
                            s.wait_ge(ev_f.h, ev_f.n)
                            ac.inc(s.nop(), 1)
                    proj_block(wb, 128, ev)
                    sy.wait_ge(ac.h, ac.n)
                    gw_f.inc(sy.dma_start(out=sg_sp[f * 128:(f + 1) * 128, :], in_=stg[:, :]), 16)

                sy.wait_ge(gw_f.h, 16 * 32)
                sy.wait_ge(sw_f.h, 16 * 32)
            barrier()
            hstk.close()
            if "p2a" in phases or "p2b" in phases:
                pcb = [p1stk.enter_context(nc.sbuf_tensor(f"pcb{i}", [128, 4096], F16))
                       for i in range(2)]
                pc_f = sem("pc_f")
                pcw_f = sem("pcw_f")
                nblk_w1 = (16 * 64 * 128) // 4096
                nblk_wo = (16 * 32 * 128) // 4096
                srcs = [(w1_in, w116_sp, i) for i in range(nblk_w1)] +                        [(wo_in, wo16_sp, j) for j in range(nblk_wo)]
                for k, (srca, dsta, i) in enumerate(srcs):
                    if k >= 2:
                        g.wait_ge(pcw_f.h, 16 * (k - 1))
                    pc_f.inc(g.dma_start(out=pcb[k % 2][:, :], in_=srca[:, i * 4096:(i + 1) * 4096]), 16)
                    g.wait_ge(pc_f.h, pc_f.n)
                    pcw_f.inc(g.dma_start(out=dsta[:, i * 4096:(i + 1) * 4096], in_=pcb[k % 2][:, :]), 16)

        prep_holder = {}
        if "prep" in phases:
            p1stk = B16_holder["stk"]
            B16 = B16_holder["B16"]

            def sb2(name, shape, dtype):
                return p1stk.enter_context(nc.sbuf_tensor(name, shape, dtype))

            def ps2(name, shape, dtype=F32):
                return p1stk.enter_context(nc.psum_tensor(name, shape, dtype))
            S_sb = sb2("S_sb", [128, DI], F32)
            S_holder["S_sb"] = S_sb
            tok2 = sb2("tok2", [128, NCH * 96], F32)
            btok = sb2("btok", [128, NCH * 128], F16)
            drep = sb2("drep", [128, NCH * H], F32)
            prep_holder.update(tok2=tok2, btok=btok, drep=drep)
            prepstk = p1stk.enter_context(ExitStack())

            def sb3(name, shape, dtype):
                return prepstk.enter_context(nc.sbuf_tensor(name, shape, dtype))
            dtraw = sb3("dtraw", [32, T], F32)
            ld.inc(sy.dma_start(out=dtraw[:, :], in_=dt_sp), 16)
            for _e in (v, s):
                _e.wait_ge(ld.h, ld.n)
            dwc = sb3("dwc", [96, T], F32)
            ztile = sb3("ztile", [32, Q], F32)
            cumE = sb3("cumE", [32, NCH], F32)
            cEtmp = sb3("cEtmp", [32, NCH], F32)
            cumET = sb3("cumET", [NCH, 32], F32)
            ustrict = sb3("ustrict", [NCH, NCH], F32)
            aa = sb3("aa", [32, T], F32)
            ee16 = sb3("ee16", [32, T], F16)
            pstp32 = prepstk.enter_context(nc.psum_tensor("pstp32", [128, 512], F32))
            pstp16 = prepstk.enter_context(nc.psum_tensor("pstp16", [128, 512], F16))

            dts = dwc[0:32, :]
            cum = dwc[32:64, :]
            ww = dwc[64:96, :]
            v.memset(ztile[:, :], 0.0)
            s.activation(aa[0:32, :], dtraw[:, :], AF.Abs, bias=dtb_sb[:, :])
            s.activation(aa[0:32, :], aa[0:32, :], AF.Exp, scale=-1.0)
            ac.inc(s.activation(aa[0:32, :], aa[0:32, :], AF.Ln, bias=1.0), 1)
            v.wait_ge(ac.h, ac.n)
            v.tensor_scalar(ww[0:32, :], dtraw[:, :], dtb_sb[:, :], None, ALU.add)
            v.tensor_scalar_max(dts, ww[0:32, :], 0.0)
            dv.inc(v.tensor_add(dts, dts, aa[0:32, :]), 1)
            dv.inc(v.tensor_scalar_mul(aa[:, :], dts, acol[:, 0:1]), 1)
            if "dbg_sp" in debug:
                sy.wait_ge(dv.h, dv.n)
                ld.inc(sy.dma_start(out=dbg_sp[0:32, 0:1], in_=acol[:, :], allow_slow_non_contiguous=True), 16)
                ld.inc(sy.dma_start(out=dbg_sp[0:32, 16:80], in_=dts[:, 0:64]), 16)
                ld.inc(sy.dma_start(out=dbg_sp[0:32, 96:160], in_=aa[:, 0:64]), 16)
                dbg_hold.append(True)
            for c in range(NCH):
                cs = slice(c * Q, (c + 1) * Q)
                v.tensor_tensor_scan(cum[:, cs], aa[0:32, cs], ztile[:, :], 0.0, ALU.add, ALU.add)
            cum_r = dwc.ap()[32:64, :].rearrange("h (c q) -> h c q", q=Q)
            dv.inc(v.tensor_copy(cumE[:, :], cum_r[:, :, Q - 1]), 1)
            gp.inc(g.memset(ustrict[:, :], 1.0), 1)
            g.wait_ge(gp.h, gp.n)
            gp.inc(g.affine_select(out=ustrict[:, :], in_=ustrict[:, :], compare_op=ALU.is_gt,
                                   fill=0.0, base=0, pattern=[[1, NCH]], channel_multiplier=-1), 1)
            t.wait_ge(dv.h, dv.n)
            pe.inc(t.transpose(pstp32[0:NCH, 0:32], cumE[:, :], ident32[0:32, 0:32]), 1)
            v.wait_ge(pe.h, pe.n)
            dv.inc(v.tensor_copy(cumET[:, :], pstp32[0:NCH, 0:32]), 1)
            t.wait_ge(dv.h, dv.n)
            t.wait_ge(gp.h, gp.n)
            pe.inc(t.matmul(pstp32[0:32, 0:NCH], cumET[:, :], ustrict[:, :],
                            start=True, stop=True, skip_group_check=True), 1)
            v.wait_ge(pe.h, pe.n)
            dv.inc(v.tensor_copy(cEtmp[:, :], pstp32[0:32, 0:NCH]), 1)
            if "dbg_sp" in debug:
                sy.wait_ge(dv.h, dv.n)
                ld.inc(sy.dma_start(out=dbg_sp[0:32, 200:232], in_=cEtmp[:, :]), 16)
                ld.inc(sy.dma_start(out=dbg_sp[0:32, 240:272], in_=cumE[:, :]), 16)
            s.wait_ge(dv.h, dv.n)
            ac.inc(s.activation(ee16[:, :], cum, AF.Exp), 1)
            sy.wait_ge(ac.h, ac.n)
            ecm_dst = e_cm.rearrange("c (h q) -> h c q", h=H)
            ld.inc(sy.dma_start(out=ecm_dst, in_=ee16.ap().rearrange("h (c q) -> h c q", q=Q)), 16)
            ccm_dst = cum_cm.rearrange("c (h q) -> h c q", h=H)
            ld.inc(sy.dma_start(out=ccm_dst, in_=cum_r), 16)
            for c in range(NCH):
                cs = slice(c * Q, (c + 1) * Q)
                wi_ = v.tensor_scalar(ww[:, cs], cum[:, cs], cumE[:, c:c + 1], None, ALU.subtract)
            dv.inc(wi_, 1)
            s.wait_ge(dv.h, dv.n)
            ac.inc(s.activation(ww, ww, AF.Exp, scale=-1.0), 1)
            v.wait_ge(ac.h, ac.n)
            for c in range(NCH):
                cs = slice(c * Q, (c + 1) * Q)
                gi_ = v.tensor_scalar(aa[0:32, cs], cum[:, cs], cEtmp[:, c:c + 1], None, ALU.add)
            dv.inc(gi_, 1)
            s.wait_ge(dv.h, dv.n)
            ac.inc(s.activation(ee16[:, :], aa[0:32, :], AF.Exp), 1)
            sy.wait_ge(ac.h, ac.n)
            egl_dst = egl_tm.rearrange("tt (h i) -> h tt i", h=H)
            ld.inc(sy.dma_start(out=egl_dst, in_=ee16.ap().rearrange("h (tt i) -> h tt i", i=TT)), 16)
            ac.inc(s.activation(cEtmp[:, :], cumE[:, :], AF.Exp), 1)
            t.wait_ge(ac.h, ac.n)
            pe.inc(t.transpose(pstp32[0:32, 0:32], cEtmp[:, :], ident32[0:32, 0:32]), 1)
            v.wait_ge(pe.h, pe.n)
            dv.inc(v.tensor_copy(ztile[0:NCH, 0:32], pstp32[0:32, 0:32]), 1)
            sy.wait_ge(dv.h, dv.n)
            ld.inc(sy.dma_start(out=drep_sp.rearrange("o (c h) -> (o c) h", h=H), in_=ztile[0:NCH, 0:32]), 16)
            sy.wait_ge(ld.h, ld.n)
            ld.inc(sy.dma_start(out=drep[:, :], in_=drep_sp[0:1, :].broadcast_to([128, NCH * H])), 16)
            t.wait_ge(dv.h, dv.n)
            t.wait_ge(ac.h, ac.n)
            for cb4 in range(NCH // 4):
                if cb4 >= 1:
                    t.wait_ge(dv.h, dv.n)
                for j in range(4):
                    c = cb4 * 4 + j
                    ii = t.transpose(pstp32[0:Q, j * 128:j * 128 + 96],
                                     dwc[:, c * Q:(c + 1) * Q], ident32[0:96, 0:96])
                pe.inc(ii, 1)
                v.wait_ge(pe.h, pe.n)
                for j in range(4):
                    c = cb4 * 4 + j
                    li = v.tensor_copy(tok2[:, c * 96:(c + 1) * 96], pstp32[0:Q, j * 128:j * 128 + 96])
                dv.inc(li, 1)
            for cb4 in range(NCH // 4):
                t.wait_ge(dv.h, dv.n)
                for j in range(4):
                    c = cb4 * 4 + j
                    ii = t.transpose(pstp16[:, j * 128:(j + 1) * 128], B16[:, c * Q:(c + 1) * Q],
                                     ident16[:, :])
                pe.inc(ii, 1)
                v.wait_ge(pe.h, pe.n)
                dv.inc(v.tensor_copy(btok[:, cb4 * 512:(cb4 + 1) * 512], pstp16[:, :]), 1)
            sy.wait_ge(ld.h, ld.n)
            barrier()
            prepstk.close()

        if "p1b" in phases:
            p1stk = B16_holder["stk"]
            B16 = B16_holder["B16"]
            S_sb = S_holder["S_sb"]
            tok2 = prep_holder["tok2"]
            btok = prep_holder["btok"]
            drep = prep_holder["drep"]
            with ExitStack() as pk:
                def sbp(name, shape, dtype):
                    return pk.enter_context(nc.sbuf_tensor(name, shape, dtype))

                def psp(name, shape, dtype=F32):
                    return pk.enter_context(nc.psum_tensor(name, shape, dtype))
                xtok = DBuf(nc, pk, "xtok", [128, DI], F16, n=2)
                ebc = DBuf(nc, pk, "ebc", [128, DI], F16, n=2)
                cbc = DBuf(nc, pk, "cbc", [128, DI], F32, n=2)
                ps_y = [psp(f"ps_y{i}", [128, TT]) for i in range(2)]
                ps_loc = [psp(f"ps_loc{i}", [128, TT]) for i in range(2)]
                ps_cb = psp("ps_cb", [128, 128])
                cbt = sbp("cbt", [128, 128], F16)
                zero128 = sbp("zero128", [128, 128], F32)
                ND = 4
                dif = [sbp(f"dif{i}", [128, 128], F32) for i in range(ND)]
                Lt = [sbp(f"Lt{i}", [128, 128], F16) for i in range(ND)]
                Mt = [sbp(f"Mt{i}", [128, 128], F16) for i in range(ND)]
                xdt = [sbp(f"xdt{i}", [128, 128], F16) for i in range(ND)]
                bw = [sbp(f"bw{i}", [128, 128], F16) for i in range(ND)]
                cw = [sbp(f"cw{i}", [128, 128], F16) for i in range(ND)]
                ystage = [sbp(f"yst{i}", [128, DI], F16) for i in range(2)]
                yw_f = [sem("yw_f0"), sem("yw_f1")]
                s16_f = sem("s16_f")
                upd_f = sem("upd_f")
                ymm_f = sem("ymm_f")
                loc_f = sem("loc_f")
                prd_f = sem("prd_f")
                dif_f = sem("dif_f")
                gprd_f = sem("gprd_f")
                yev_f = sem("yev_f")
                cbt_f = sem("cbt_f")

                v.memset(S_sb[:, :], 0.0)
                dv.inc(v.memset(zero128[:, :], 0.0), 1)

                def issue_loads(c):
                    xtok.prod_wait(sy)
                    xtok.prod_done(sy.dma_start(out=xtok.prod_buf()[:, :],
                                                in_=xc_sp[:, c * Q:(c + 1) * Q], transpose=True))
                    ebc.prod_wait(sy)
                    ebc.prod_done(sy.dma_start(out=ebc.prod_buf()[:, :],
                                               in_=e_cm[c:c + 1, :].broadcast_to([128, H * Q])))
                    cbc.prod_wait(sy)
                    cbc.prod_done(sy.dma_start(out=cbc.prod_buf()[:, :],
                                               in_=cum_cm[c:c + 1, :].broadcast_to([128, H * Q])))

                issue_loads(0)
                issue_loads(1)
                gprd_hist = {}
                loc_hist = {}
                prd_hist = {}
                yev_hist = {}
                upd_hist = {}
                gidx = 0
                gprd_hist = {}
                import os
                CHUNK_BARRIER = os.environ.get("KL_CHUNK_BARRIER", "0") == "1"
                for c in range(NCH):
                    if CHUNK_BARRIER:
                        barrier()
                    cs = slice(c * Q, (c + 1) * Q)
                    xT = xtok.cons_buf()
                    eT = ebc.cons_buf()
                    cT = cbc.cons_buf()
                    t.wait_ge(dv.h, dv.n)
                    t.wait_ge(prd_f.h, prd_f.n)
                    t.wait_ge(cbt_f.h, cbt_f.n)
                    pe.inc(t.matmul(ps_cb[:, :], B16[:, cs], C16[:, cs], start=True, stop=True,
                                    skip_group_check=True), 1)
                    v.wait_ge(pe.h, pe.n)
                    v.wait_ge(gprd_f.h, gprd_f.n)
                    cbt_f.inc(v.tensor_mul(cbt[:, :], ps_cb[:, :], trimask[:, :]), 1)
                    if c >= 1:
                        v.wait_ge(ymm_f.h, ymm_f.n)
                    s16_f.inc(v.tensor_copy(S16[:, :], S_sb[:, :]), 1)
                    xtok.cons_wait(v)
                    ebc.cons_wait(g)
                    cbc.cons_wait(v)
                    g.wait_ge(cbt_f.h, cbt_f.n)
                    for h in range(H):
                        hs = slice(h * 128, (h + 1) * 128)
                        dd = h % ND
                        dtc = tok2[:, c * 96 + h: c * 96 + h + 1]
                        cumc = tok2[:, c * 96 + 32 + h: c * 96 + 32 + h + 1]
                        wc = tok2[:, c * 96 + 64 + h: c * 96 + 64 + h + 1]
                        key2 = (c * H + h) - ND
                        if key2 >= 0:
                            v.wait_ge(loc_f.h, loc_hist[key2])
                        dif_f.inc(v.scalar_tensor_tensor(dif[dd][:, :], cT[:, hs], cumc,
                                                         zero128[:, :], ALU.subtract, ALU.min), 1)
                        if key2 >= 0:
                            s.wait_ge(gprd_f.h, gprd_hist[key2])
                        s.wait_ge(dif_f.h, dif_f.n)
                        ac.inc(s.activation(Lt[dd][:, :], dif[dd][:, :], AF.Exp), 1)
                        v.tensor_scalar_mul(xdt[dd][:, :], xT[:, hs], dtc)
                        prd_f.inc(v.tensor_scalar_mul(bw[dd][:, :], btok[:, c * 128:(c + 1) * 128], wc), 1)
                        prd_hist[c * H + h] = prd_f.n
                        if key2 >= 0:
                            g.wait_ge(loc_f.h, loc_hist[key2])
                        g.wait_ge(ac.h, ac.n)
                        g.tensor_mul(Mt[dd][:, :], Lt[dd][:, :], cbt[:, :])
                        gprd_f.inc(g.tensor_mul(cw[dd][:, :], C16[:, cs], eT[:, hs]), 1)
                        gprd_hist[c * H + h] = gprd_f.n
                        grp = (h // 4) % 2
                        slot = slice((h % 4) * 128, (h % 4 + 1) * 128)
                        t.wait_ge(prd_f.h, prd_f.n)
                        t.wait_ge(gprd_f.h, gprd_f.n)
                        if h % 4 == 0:
                            if gidx >= 2:
                                t.wait_ge(yev_f.h, yev_hist[gidx - 2])
                                t.wait_ge(upd_f.h, upd_hist[gidx - 2])
                        t.wait_ge(s16_f.h, s16_f.n)
                        t.matmul(ps_y[grp][:, slot], xdt[dd][:, :], Mt[dd][:, :], start=True, stop=False,
                                 skip_group_check=True)
                        ymm_f.inc(t.matmul(ps_y[grp][:, slot], S16[:, hs], cw[dd][:, :], start=False,
                                           stop=True, skip_group_check=True), 1)
                        loc_f.inc(t.matmul(ps_loc[grp][:, slot], bw[dd][:, :], xdt[dd][:, :], start=True,
                                           stop=True, skip_group_check=True), 1)
                        loc_hist[c * H + h] = loc_f.n
                        if h % 4 == 3:
                            v.wait_ge(loc_f.h, loc_f.n)
                            for hq in range(h - 3, h + 1):
                                hqs = slice(hq * 128, (hq + 1) * 128)
                                sq_ = slice((hq % 4) * 128, (hq % 4 + 1) * 128)
                                upd_f.inc(v.scalar_tensor_tensor(S_sb[:, hqs], S_sb[:, hqs],
                                                                 drep[:, c * H + hq: c * H + hq + 1],
                                                                 ps_loc[grp][:, sq_], ALU.mult, ALU.add), 1)
                            s.wait_ge(ymm_f.h, ymm_f.n)
                            if c >= 2 and h == 3:
                                yw_f[c % 2].wait(s)
                            ins = s.copy(ystage[c % 2][:, (h - 3) * 128:(h + 1) * 128], ps_y[grp][:, :])
                            yev_f.inc(ins, 1)
                            upd_hist[gidx] = upd_f.n
                            yev_hist[gidx] = yev_f.n
                            gidx += 1
                    v.wait_ge(prd_f.h, prd_f.n)
                    xtok.cons_done(v.engine_nop())
                    cbc.cons_done(v.engine_nop())
                    g.wait_ge(gprd_f.h, gprd_f.n)
                    ebc.cons_done(g.engine_nop())
                    if c + 2 < NCH:
                        issue_loads(c + 2)
                    sy.wait_ge(yev_f.h, yev_f.n)
                    ydst = yl_sp.rearrange("(hh p) t -> p hh t", p=128)[:, :, c * Q:(c + 1) * Q]
                    ysrc = ystage[c % 2].ap().rearrange("p (hh i) -> p hh i", hh=H)
                    yw_f[c % 2].inc(sy.dma_start(out=ydst, in_=ysrc), 16)
                yw_f[0].wait(sy)
                yw_f[1].wait(sy)
            barrier()

        if "hand" in phases:
            S_sb = S_holder["S_sb"]
            with ExitStack() as pk:
                mselbc = pk.enter_context(nc.sbuf_tensor("mselbc", [128, DI], F32))
                hw_f = sem("hw_f")
                hp_f = sem("hp_f")
                g.wait_ge(dv.h, dv.n)
                with g.register() as rwoff:
                    g.load(rwoff, ctl_sb[0:1, 0:1])
                    woff = g.compute_val(rwoff)
                    dst = dataclasses.replace(shs[0:128, :], offset=woff * DI)
                    hw_f.inc(g.dma_start(out=dst, in_=S_sb[:, :]), 16)
                g.wait_ge(hw_f.h, hw_f.n)
                with g.register() as rfoff:
                    g.load(rfoff, ctl_sb[0:1, 1:2])
                    foff = g.compute_val(rfoff)
                    dstf = dataclasses.replace(shf[0:1, :], offset=foff * 64)
                    hw_f.inc(g.dma_start(out=dstf, in_=nonce_sb[0:1, :]), 16)
                gp.inc(g.engine_nop(), 1)
                sy.wait_ge(gp.h, gp.n)
                with sy.register() as rn, sy.register() as rm, sy.register() as rf:
                    sy.load(rn, ctl_sb[0:1, 2:3])
                    sy.load(rm, ctl_sb[0:1, 3:4])
                    hp_f.inc(sy.dma_start(out=flag_sb[0:1, :], in_=shf[0:1, :]), 16)
                    sy.wait_ge(hp_f.h, hp_f.n)
                    sy.load(rf, flag_sb[0:1, 0:1].bitcast(I32))
                    nvv = sy.compute_val(rn)
                    mvv = sy.compute_val(rm)
                    with sy.While(lambda: (sy.compute_val(rf) != nvv) & (mvv != 0)):
                        sy.sem_clear(hp_f.h)
                        sy.dma_start(out=flag_sb[0:1, :], in_=shf[0:1, :]).then_inc(hp_f.h, 16)
                        sy.wait_ge(hp_f.h, 16)
                        sy.load(rf, flag_sb[0:1, 0:1].bitcast(I32))
                    hp_f.n = 16
                ld.inc(sy.dma_start(out=S_sb[:, :], in_=shs[0:128, :]), 16)
                ld.inc(sy.dma_start(out=mselbc[:, :], in_=msel_in[0:1, :].broadcast_to([128, DI])), 16)
                v.wait_ge(ld.h, ld.n)
                v.tensor_mul(S_sb[:, :], S_sb[:, :], mselbc[:, :])
                dv.inc(v.tensor_copy(S16[:, :], S_sb[:, :]), 1)
            barrier()
            B16_holder["stk"].close()

        if "p2a" in phases:
            with ExitStack() as pk:
                def sbp(name, shape, dtype):
                    return pk.enter_context(nc.sbuf_tensor(name, shape, dtype))

                def psp(name, shape, dtype=F32):
                    return pk.enter_context(nc.psum_tensor(name, shape, dtype))
                psF = [psp(f"psF{i}", [128, TT]) for i in range(8)]
                big_a = sbp("big_a", [128, KC * T], F16)
                xl_f = sem("xnl_f")
                xn_src = xn_sp.rearrange("(kc p) t -> p kc t", p=128)
                xl_f.inc(sy.dma_start(out=big_a.ap().rearrange("p (kc t) -> p kc t", kc=KC),
                                      in_=xn_src), 16)
                pka = pk.enter_context(ExitStack())
                wbufA = DBuf(nc, pka, "wbufA", [128, KC * 128], F16, n=2)
                dstage = [pka.enter_context(nc.sbuf_tensor(f"dstA{i}", [128, T], F16)) for i in range(2)]
                tile2_f = sem("tile2_f")
                ev2_f = sem("ev2_f")
                dw_f = [sem("dw_f0"), sem("dw_f1")]
                t.wait_ge(xl_f.h, xl_f.n)
                s.wait_ge(xl_f.h, xl_f.n)
                for n in range(64):
                    wbufA.prod_wait(g)
                    wb = wbufA.prod_buf()
                    wbufA.prod_done(g.dma_start(out=wb[:, :], in_=w0_in[:, n * KC * 128:(n + 1) * KC * 128]))
                    wbufA.cons_wait(t)
                    base_t = tile2_f.n
                    stg = dstage[n % 2]
                    for tl in range(NTT):
                        t.wait_ge(ev2_f.h, max(0, ev2_f.n - 8 + tl + 1))
                        for kc in range(KC):
                            mm = t.matmul(psF[tl][:, :], wb[:, kc * 128:(kc + 1) * 128],
                                          big_a[:, kc * T + tl * TT: kc * T + (tl + 1) * TT],
                                          start=(kc == 0), stop=(kc == KC - 1), skip_group_check=True)
                        tile2_f.inc(mm, 1)
                    t.wait_ge(tile2_f.h, tile2_f.n)
                    wbufA.cons_done(t.nop())
                    for tl in range(NTT):
                        s.wait_ge(tile2_f.h, base_t + tl + 1)
                        if tl == 0 and n >= 2:
                            dw_f[n % 2].wait(s)
                        ins = s.activation(stg[:, tl * TT:(tl + 1) * TT], psF[tl][:, :],
                                           AF.Gelu, bias=b0_sb[:, n:n + 1])
                        ev2_f.inc(ins, 1)
                        if tl == NTT - 1:
                            s.wait_ge(ev2_f.h, ev2_f.n)
                            ac.inc(s.nop(), 1)
                    sy.wait_ge(ac.h, ac.n)
                    dw_f[n % 2].inc(sy.dma_start(out=d0_sp[n * 128:(n + 1) * 128, :], in_=stg[:, :]), 16)
                dw_f[0].wait(sy)
                dw_f[1].wait(sy)
                barrier()
                pka.close()
                if "noffnb" in phases:
                    return nc

                w1buf = DBuf(nc, pk, "w1buf", [128, 64 * 128], F16, n=2)
                gst2 = [sbp(f"gstB{i}", [128, 1024], F16) for i in range(2)]
                dl_f = sem("dl_f")
                gw2_f = [sem("gw2_f0"), sem("gw2_f1")]
                tile3_f = sem("tile3_f")
                ev3_f = sem("ev3_f")
                KB = 64
                for t4 in range(4):
                    if t4 >= 1:
                        sy.wait_ge(tile3_f.h, tile3_f.n)
                    d_dst = big_a.ap().rearrange("p (kc i) -> p kc i", kc=KB)
                    d_src = d0_sp.rearrange("(kc p) t -> p kc t", p=128)[:, :, t4 * 1024:(t4 + 1) * 1024]
                    dl_f.inc(sy.dma_start(out=d_dst, in_=d_src), 16)
                    t.wait_ge(dl_f.h, dl_f.n)
                    for m in range(16):
                        w1buf.prod_wait(g)
                        wb1 = w1buf.prod_buf()
                        w1buf.prod_done(g.dma_start(out=wb1[:, :],
                                                    in_=w116_sp[:, m * KB * 128:(m + 1) * KB * 128]))
                        w1buf.cons_wait(t)
                        base3 = tile3_f.n
                        bank0 = (m % 4) * 2
                        for hh in range(2):
                            t.wait_ge(ev3_f.h, max(0, ev3_f.n - 8 + hh + 1))
                            for kc in range(KB):
                                mm = t.matmul(psF[bank0 + hh][:, :], wb1[:, kc * 128:(kc + 1) * 128],
                                              big_a[:, kc * 1024 + hh * TT: kc * 1024 + hh * TT + TT],
                                              start=(kc == 0), stop=(kc == KB - 1), skip_group_check=True)
                            tile3_f.inc(mm, 1)
                        t.wait_ge(tile3_f.h, tile3_f.n)
                        w1buf.cons_done(t.nop())
                        stg2 = gst2[m % 2]
                        for hh in range(2):
                            s.wait_ge(tile3_f.h, base3 + hh + 1)
                            if hh == 0 and (t4 * 16 + m) >= 2:
                                gw2_f[m % 2].wait(s)
                            ins = s.activation(stg2[:, hh * TT:(hh + 1) * TT], psF[bank0 + hh][:, :],
                                               AF.Silu, bias=b1_sb[:, m:m + 1])
                            ev3_f.inc(ins, 1)
                            if hh == 1:
                                s.wait_ge(ev3_f.h, ev3_f.n)
                                ac.inc(s.nop(), 1)
                        sy.wait_ge(ac.h, ac.n)
                        gw2_f[m % 2].inc(sy.dma_start(out=g_sp[m * 128:(m + 1) * 128, t4 * 1024:(t4 + 1) * 1024],
                                                      in_=stg2[:, :]), 16)
                gw2_f[0].wait(sy)
                gw2_f[1].wait(sy)
            barrier()

        if "p2b" in phases:
            with ExitStack() as pk:
                def sbp(name, shape, dtype):
                    return pk.enter_context(nc.sbuf_tensor(name, shape, dtype))

                def psp(name, shape, dtype=F32):
                    return pk.enter_context(nc.psum_tensor(name, shape, dtype))
                ps_z = [psp(f"ps_z{i}", [128, TT]) for i in range(2)]
                ps_o = [psp(f"ps_o{i}", [128, TT]) for i in range(2)]
                big_b = sbp("big_b", [128, 4 * H * TT], F16)
                ylocv = big_b[:, 0:H * TT]
                xcvv = big_b[:, H * TT:2 * H * TT]
                sgtv = big_b[:, 2 * H * TT:3 * H * TT]
                yfinv = big_b[:, 3 * H * TT:4 * H * TT]
                eglb = DBuf(nc, pk, "eglb", [128, 8 * TT], F16, n=2)
                wobuf = DBuf(nc, pk, "wobuf", [128, 32 * 128], F16, n=2)
                xnm = [sbp(f"xnm{i}", [128, TT], F16) for i in range(2)]
                gm = [sbp(f"gm{i}", [128, TT], F16) for i in range(2)]
                xm = [sbp(f"xm{i}", [128, TT], F32) for i in range(2)]
                ost = [sbp(f"ost{i}", [128, TT], F32) for i in range(2)]
                cwz = sbp("cwz", [128, TT], F16)
                ya1 = sbp("ya1", [128, TT], F16)
                tl_f = sem("tl_f")
                zf_f = sem("zf_f")
                asm_f = sem("asm_f")
                omm_f = sem("omm_f")
                mload_f = sem("mload_f")
                ow_f = [sem("ow_f0"), sem("ow_f1")]
                asm_hist = {}
                dvm_hist = {}
                mg = 0
                def big3(tt2):
                    tts2 = slice(tt2 * TT, (tt2 + 1) * TT)
                    if tt2 >= 1:
                        sy.wait_ge(asm_f.h, 32 * tt2)
                    for (dstv, sp) in [(ylocv, yl_sp), (xcvv, xc_sp), (sgtv, sg_sp)]:
                        dd = dstv.rearrange("p (hh i) -> p hh i", hh=H)
                        ss_ = sp.rearrange("(hh p) t -> p hh t", p=128)[:, :, tts2]
                        tl_f.inc(sy.dma_start(out=dd, in_=ss_), 16)

                big3(0)
                for tt in range(NTT):
                    tts = slice(tt * TT, (tt + 1) * TT)
                    for gq in range(4):
                        eglb.prod_wait(sy)
                        src = egl_tm[tt:tt + 1, gq * 8 * TT:(gq + 1) * 8 * TT].broadcast_to([128, 8 * TT])
                        eglb.prod_done(sy.dma_start(out=eglb.prod_buf()[:, :], in_=src))
                    v.wait_ge(tl_f.h, 48 * (tt + 1))
                    for h in range(H):
                        hs = slice(h * 128, (h + 1) * 128)
                        hts = slice(h * TT, (h + 1) * TT)
                        if h % 8 == 0:
                            eglb.cons_wait(v)
                        eb = eglb.cons_buf()
                        v.wait_ge(zf_f.h, zf_f.n)
                        dv.inc(v.tensor_mul(cwz[:, :], C16[:, tts], eb[:, (h % 8) * TT:(h % 8 + 1) * TT]), 1)
                        t.wait_ge(dv.h, dv.n)
                        gk = tt * H + h
                        if gk >= 2:
                            t.wait_ge(asm_f.h, asm_hist[gk - 2])
                        zf_f.inc(t.matmul(ps_z[h % 2][:, :], S16[:, hs], cwz[:, :],
                                          start=True, stop=True, skip_group_check=True), 1)
                        v.scalar_tensor_tensor(ya1[:, :], xcvv[:, hts], dvec_sb[:, h:h + 1],
                                               ylocv[:, hts], ALU.mult, ALU.add)
                        v.wait_ge(zf_f.h, zf_f.n)
                        v.tensor_add(ya1[:, :], ya1[:, :], ps_z[h % 2][:, :])
                        asm_f.inc(v.tensor_mul(yfinv[:, hts], ya1[:, :], sgtv[:, hts]), 1)
                        asm_hist[gk] = asm_f.n
                        if h % 8 == 7:
                            v.wait_ge(asm_f.h, asm_f.n)
                            eglb.cons_done(v.engine_nop())
                    if tt + 1 < NTT:
                        big3(tt + 1)
                    for m in range(16):
                        wobuf.prod_wait(g)
                        wob = wobuf.prod_buf()
                        wobuf.prod_done(g.dma_start(out=wob[:, :],
                                                    in_=wo16_sp[:, m * 32 * 128:(m + 1) * 32 * 128]))
                        mb = m % 2
                        if mg >= 2:
                            sy.wait_ge(dv.h, dvm_hist[mg - 2])
                        mload_f.inc(sy.dma_start(out=xnm[mb][:, :], in_=xn_sp[m * 128:(m + 1) * 128, tts]), 16)
                        mload_f.inc(sy.dma_start(out=gm[mb][:, :], in_=g_sp[m * 128:(m + 1) * 128, tts]), 16)
                        mload_f.inc(sy.dma_start(out=xm[mb][:, :], in_=x_in[m * 128:(m + 1) * 128, tts]), 16)
                        wobuf.cons_wait(t)
                        t.wait_ge(asm_f.h, asm_f.n)
                        if mg >= 2:
                            t.wait_ge(dv.h, dvm_hist[mg - 2])
                        for kc in range(32):
                            mm = t.matmul(ps_o[mb][:, :], wob[:, kc * 128:(kc + 1) * 128],
                                          yfinv[:, kc * TT:(kc + 1) * TT],
                                          start=(kc == 0), stop=(kc == 31), skip_group_check=True)
                        omm_f.inc(mm, 1)
                        t.wait_ge(omm_f.h, omm_f.n)
                        wobuf.cons_done(t.nop())
                        v.wait_ge(omm_f.h, omm_f.n)
                        v.wait_ge(mload_f.h, mload_f.n)
                        if mg >= 2:
                            ow_f[mb].wait(v)
                        v.tensor_add(ost[mb][:, :], ps_o[mb][:, :], xnm[mb][:, :])
                        v.tensor_mul(ost[mb][:, :], ost[mb][:, :], gm[mb][:, :])
                        dv.inc(v.tensor_add(ost[mb][:, :], ost[mb][:, :], xm[mb][:, :]), 1)
                        dvm_hist[mg] = dv.n
                        mg += 1
                        sy.wait_ge(dv.h, dv.n)
                        ow_f[mb].inc(sy.dma_start(out=out_o[m * 128:(m + 1) * 128, tts], in_=ost[mb][:, :]), 16)
                ow_f[0].wait(sy)
                ow_f[1].wait(sy)
            barrier()
        else:
            pass

        for e in ALL_ENG:
            e.wait_ge(bar.h, bar.n)

    return nc


def prep_core_inputs(core, inputs, nonce):
    b, half = core // 2, core % 2
    x_full = inputs["L0_attn_x"]
    xs = np.ascontiguousarray(x_full[b, half * T:(half + 1) * T, :].T.astype(np.float32))
    if half == 0:
        x3 = np.zeros((D, 3), np.float32)
    else:
        x3 = np.ascontiguousarray(x_full[b, half * T - 3: half * T, :].T.astype(np.float32))
    parity = half
    nonce_bits = int(nonce[0:1, 0].view(np.int32)[0])
    ctl = np.array([[parity * 128, parity, nonce_bits, parity, 0, 0, 0, 0]], dtype=np.int32)
    msel = np.full((1, DI), float(parity), np.float32)
    return dict(x=xs, x3=x3, ctl=ctl, noncef=nonce, msel=msel)


def prep_shared_inputs(inputs):
    BLK_OFF, WI_COLS = block_col_offsets()
    Wi = np.asarray(inputs["L0_attn_in_proj_w"], np.float32)
    wi = np.zeros((128, WI_COLS), np.float32)
    for (name, n0, ncols) in IN_BLOCKS:
        (o, _) = BLK_OFF[name]
        blk = Wi[:, n0:n0 + ncols].reshape(KC, 128, ncols)
        wi[:, o:o + KC * ncols] = blk.transpose(1, 0, 2).reshape(128, KC * ncols)

    def pack(W, nchunks, kcc):
        out = np.zeros((128, nchunks * kcc * 128), np.float32)
        for n in range(nchunks):
            blk = W[:, n * 128:(n + 1) * 128].reshape(kcc, 128, 128)
            out[:, n * kcc * 128:(n + 1) * kcc * 128] = blk.transpose(1, 0, 2).reshape(128, kcc * 128)
        return out

    W0T = np.ascontiguousarray(np.asarray(inputs["L0_ffn_W0"], np.float32).T)
    W1T = np.ascontiguousarray(np.asarray(inputs["L0_ffn_W1"], np.float32).T)
    Wo = np.asarray(inputs["L0_attn_out_proj_w"], np.float32)
    w0 = pack(W0T, 64, KC)
    w1 = pack(W1T, 16, 64)
    wo = pack(Wo, 16, 32)
    cw = np.asarray(inputs["L0_attn_conv_weight"], np.float32)
    convw = cw.T.reshape(32, 128, 4).transpose(1, 0, 2).reshape(128, 128)
    convb = np.asarray(inputs["L0_attn_conv_bias"], np.float32).reshape(32, 128).T.copy()
    dvec = np.asarray(inputs["L0_attn_D"], np.float32).reshape(32, 128).T.copy()
    scale = np.asarray(inputs["L0_attn_norm_scale"], np.float32).reshape(KC, 128).T.copy()
    b0 = np.asarray(inputs["L0_ffn_bias0"], np.float32).reshape(64, 128).T.copy()
    b1 = np.asarray(inputs["L0_ffn_bias1"], np.float32).reshape(16, 128).T.copy()
    alog = np.asarray(inputs["L0_attn_A_log"], np.float32).reshape(32, 1)
    dtb = np.asarray(inputs["L0_attn_dt_bias"], np.float32).reshape(32, 1)
    return dict(wi=wi, w0=w0, w1=w1, wo=wo, convw=convw, convb=convb, dvec=dvec,
                scale=scale, b0=b0, b1=b1, alog=alog, dtb=dtb)


def make_in_maps(inputs):
    import os
    nonce = np.frombuffer(os.urandom(256), dtype=np.float32).reshape(1, 64).copy()
    shared = prep_shared_inputs(inputs)
    maps = []
    for c in range(8):
        m = dict(shared)
        m.update(prep_core_inputs(c, inputs, nonce))
        maps.append(m)
    return maps


def assemble_output(results):
    out = np.empty((4, 8192, D), np.float32)
    for c in range(8):
        b, half = c // 2, c % 2
        out[b, half * T:(half + 1) * T, :] = results[c]["out"].T
    return out


from concourse.bass_utils import run_bass_kernel_spmd as _run_spmd


def kernel(**inputs):
    maps = make_in_maps(inputs)
    nc = build_program()
    res = _run_spmd(nc, maps, core_ids=list(range(8)))
    return assemble_output(res.results)
